# version 02 (final)
# baseline (speedup 1.0000x reference)
import contextlib

import numpy as np

import concourse.bass as bass
import concourse.mybir as mybir
from concourse.bass_utils import run_bass_kernel_spmd

F32 = mybir.dt.float32
BF16 = mybir.dt.bfloat16
I32 = mybir.dt.int32
AF = mybir.ActivationFunctionType
ALU = mybir.AluOpType
AX = mybir.AxisListType.X

NCORES = 8
SEQ = 2048
D = 1024
NCH = 8
NH = 16
BAND = 512
NBAND = SEQ // BAND
NTT = SEQ // 128
DFF = 3072
NMC = DFF // 128
TTOK = 2 * SEQ
NGB = TTOK // BAND
QMAX = 32767.0


class SB:
    def __init__(self):
        self.h = {}
        self.c = {}

    def add(self, name, handle):
        self.h[name] = handle
        self.c[name] = 0

    def inc(self, instr, name, n=1):
        instr.then_inc(self.h[name], n)
        self.c[name] += n
        return instr

    def wait(self, eng, name, val=None):
        eng.wait_ge(self.h[name], self.c[name] if val is None else val)


class Rot:

    def __init__(self, sb, items):
        self.sb = sb
        self.items = list(items)
        self.base = [sb.c[s] for _, s in self.items]
        self.uses = [0] * len(self.items)
        self.i = 0

    def acquire(self, eng):
        j = self.i % len(self.items)
        self.i += 1
        bank, s = self.items[j]
        if self.uses[j] > 0:
            eng.wait_ge(self.sb.h[s], self.base[j] + self.uses[j])
        self.uses[j] += 1
        return bank, s

    def peek(self, k):
        return self.items[k % len(self.items)]


def _build(stop='full'):
    nc = bass.Bass("TRN2", target_bir_lowering=False, detect_race_conditions=False)

    x_in = nc.dram_tensor("x", [2, D, SEQ], F32, kind="ExternalInput")
    y_out = nc.dram_tensor("y", [2, D, SEQ], F32, kind="ExternalOutput")
    wq0 = nc.dram_tensor("wq0", [D, D], F32, kind="ExternalInput")
    wk0 = nc.dram_tensor("wk0", [D, D], F32, kind="ExternalInput")
    wv0 = nc.dram_tensor("wv0", [D, D], F32, kind="ExternalInput")
    wo0 = nc.dram_tensor("wo0", [D, D], F32, kind="ExternalInput")
    bq0 = nc.dram_tensor("bq0", [128, NCH], F32, kind="ExternalInput")
    bk0 = nc.dram_tensor("bk0", [128, NCH], F32, kind="ExternalInput")
    bv0r = nc.dram_tensor("bv0r", [1, D], F32, kind="ExternalInput")
    wg_nat = nc.dram_tensor("wgT", [DFF, D], F32, kind="ExternalInput")
    wu_nat = nc.dram_tensor("wuT", [DFF, D], F32, kind="ExternalInput")
    wd_nat = nc.dram_tensor("wdT", [D, DFF], F32, kind="ExternalInput")
    wq1 = nc.dram_tensor("wq1", [D, D], F32, kind="ExternalInput")
    wk1 = nc.dram_tensor("wk1", [D, D], F32, kind="ExternalInput")
    wv1 = nc.dram_tensor("wv1", [D, D], F32, kind="ExternalInput")
    rmbd_d = nc.dram_tensor("rmbd", [4, 8, 128, 128], F32, kind="ExternalInput")
    rupbd_d = nc.dram_tensor("rupbd", [4, 8, 4, 128, 128], F32, kind="ExternalInput")
    wcT = nc.dram_tensor("wcT", [16 * D, D], F32, kind="ExternalInput")

    resA = nc.dram_tensor("resA", [2, D, SEQ], F32, kind="Internal")
    resB = nc.dram_tensor("resB", [2, D, SEQ], F32, kind="Internal")
    resC = nc.dram_tensor("resC", [2, D, SEQ], F32, kind="Internal")
    up_d = nc.dram_tensor("up_d", [NMC, 128, TTOK], BF16, kind="Internal")
    gate_d = nc.dram_tensor("gate_d", [NMC, 128, TTOK], BF16, kind="Internal")
    prod_d = nc.dram_tensor("prod_d", [NMC, 128, TTOK], BF16, kind="Internal")
    o_d = nc.dram_tensor("o_d", [NMC, 128, TTOK], BF16, kind="Internal")
    red_d = nc.dram_tensor("red_d", [128, 1], F32, kind="Internal")
    cc_in = [nc.dram_tensor(f"cc_in{k}", [1, 16], F32, kind="Internal") for k in range(4)]
    cc_out = [nc.dram_tensor(f"cc_out{k}", [1, 16], F32, kind="Internal") for k in range(4)]

    sb = SB()
    st = contextlib.ExitStack()

    def sbuf(name, shape, dt):
        return st.enter_context(nc.sbuf_tensor(name, shape, dt))

    with st:
        for s in ("sd", "gd", "v", "sc", "mm", "stat", "exp", "pvk", "rec", "msk",
                  "qk0", "qk1", "sa0", "sa1", "sa2", "sa3", "pb0", "pb1", "cb",
                  "cc", "f0", "f1", "f2", "f3"):
            sb.add(s, st.enter_context(nc.semaphore(s)))

        ones_st = sbuf("ones_st", [128, 1], BF16)
        ones_tok = sbuf("ones_tok", [1, 128], BF16)
        ones_f = sbuf("ones_f", [1, 128], F32)
        eps_t = sbuf("eps_t", [1, 1], F32)
        masks = sbuf("masks", [128, 4, BAND], BF16)
        bq_t = sbuf("bq_t", [128, NCH], F32)
        bk_t = sbuf("bk_t", [128, NCH], F32)
        bvrow = sbuf("bvrow", [1, D], BF16)
        ident_bf = sbuf("ident_bf", [128, 128], BF16)
        scal = sbuf("scal", [128, 16], F32)
        srow = sbuf("srow", [1, 16], F32)
        redrow = sbuf("redrow", [1, 128], F32)
        gmax_t = sbuf("gmax_t", [128, 1], F32)
        amax_t = sbuf("amax_t", [128, 1], F32)
        red_buf = sbuf("red_buf", [128, 512], F32)

        with nc.Block() as blk:
            @blk.sync
            def _(e):
                sb.inc(e.dma_start(out=bq_t[:, :], in_=bq0.ap()), "sd", 16)
                sb.inc(e.dma_start(out=bk_t[:, :], in_=bk0.ap()), "sd", 16)

            @blk.gpsimd
            def _(e):
                sb.inc(e.dma_start(out=bvrow[:, :], in_=bv0r.ap()), "gd", 16)

            @blk.vector
            def _(e):
                e.memset(ident_bf[:, :], 1.0)
                e.memset(ones_st[:, :], 1.0)
                e.memset(ones_tok[:, :], 1.0)
                e.memset(ones_f[:, :], 1.0)
                e.memset(eps_t[:, :], 1e-5)
                e.memset(masks[:, :, :], 1.0)

        with nc.Block() as blk:
            @blk.gpsimd
            def _(e):
                for j in range(4):
                    e.affine_select(masks[:, j, :], masks[:, j, :], [[1, BAND]],
                                    compare_op=ALU.is_ge, fill=0.0,
                                    base=-128 * j, channel_multiplier=-1)
                e.affine_select(ident_bf[:, :], ident_bf[:, :], [[1, 128]],
                                compare_op=ALU.is_equal, fill=0.0,
                                base=0, channel_multiplier=-1)

        def emit_ln_band(blk, src_dram, s, t0, T, xband, xb_bf, sq2, xln,
                         ps_sx, ps_sxx, ps_bc0, ps_bc1,
                         murow, varrow, rstdrow, mubc, rstdbc, xcol0=0):
            @blk.sync
            def _(e):
                sb.inc(e.dma_start(
                    out=xband[:, :, :],
                    in_=src_dram.ap()[s, :, t0:t0 + T]
                        .rearrange("(c p) t -> p c t", p=128)), "sd", 16)

            sd_ld = sb.c["sd"]
            mm_base = sb.c["mm"]
            v_base = sb.c["v"]

            @blk.vector
            def _(e):
                sb.wait(e, "sd", sd_ld)
                for c in range(NCH):
                    if c >= 2:
                        e.wait_ge(sb.h["mm"], mm_base + (c - 1))
                    e.tensor_copy(xb_bf[:, c, :], xband[:, c, :])
                    sb.inc(e.tensor_mul(sq2[:, c % 2, :], xb_bf[:, c, :],
                                        xb_bf[:, c, :]), "v")

            @blk.tensor
            def _(e):
                for c in range(NCH):
                    e.wait_ge(sb.h["v"], v_base + c + 1)
                    e.matmul(ps_sx[0:1, :], ones_st[:, :], xb_bf[:, c, :],
                             start=(c == 0), stop=(c == NCH - 1))
                    sb.inc(e.matmul(ps_sxx[0:1, :], ones_st[:, :], sq2[:, c % 2, :],
                                    start=(c == 0), stop=(c == NCH - 1)), "mm")

            mm_stats = sb.c["mm"]
            v2 = sb.c["v"]

            @blk.vector
            def _(e):
                e.wait_ge(sb.h["mm"], mm_stats)
                e.tensor_scalar_mul(murow[:, :], ps_sx[0:1, :], 1.0 / D)
                e.tensor_mul(varrow[:, :], murow[:, :], murow[:, :])
                sb.inc(e.scalar_tensor_tensor(varrow[:, :], ps_sxx[0:1, :], 1.0 / D,
                                              varrow[:, :], op0=ALU.mult,
                                              op1=ALU.subtract), "v")

            sc_b = sb.c["sc"]

            @blk.scalar
            def _(e):
                e.wait_ge(sb.h["v"], v2 + 1)
                e.activation(varrow[:, :], varrow[:, :], AF.Ln, bias=eps_t[:, :], scale=1.0)
                sb.inc(e.activation(rstdrow[:, :], varrow[:, :], AF.Exp,
                                    bias=0.0, scale=-0.5), "sc")

            mm2 = sb.c["mm"]

            @blk.tensor
            def _(e):
                e.wait_ge(sb.h["sc"], sc_b + 1)
                e.matmul(ps_bc0[:, :], ones_f[:, :], murow[:, :], start=True, stop=True)
                sb.inc(e.matmul(ps_bc1[:, :], ones_f[:, :], rstdrow[:, :],
                                start=True, stop=True), "mm")

            v3 = sb.c["v"]

            @blk.vector
            def _(e):
                e.wait_ge(sb.h["mm"], mm2 + 1)
                e.tensor_copy(mubc[:, :], ps_bc0[:, :])
                e.tensor_copy(rstdbc[:, :], ps_bc1[:, :])
                for c in range(NCH):
                    e.tensor_sub(xband[:, c, :], xband[:, c, :], mubc[:, :])
                    ii = e.tensor_mul(xln[:, c, xcol0:xcol0 + T], xband[:, c, :],
                                      rstdbc[:, :])
                sb.inc(ii, "v")

            return v3 + 1

        def attn_layer(layer, s, src_dram, dst_dram):
            l0 = (layer == 0)
            wq, wk, wv = (wq0, wk0, wv0) if l0 else (wq1, wk1, wv1)
            tag = f"_{layer}{s}"
            with contextlib.ExitStack() as stL:
                def sbL(name, shape, dt):
                    return stL.enter_context(nc.sbuf_tensor(name + tag, shape, dt))

                q_bf = sbL("q_bf", [128, NCH, SEQ], BF16)
                k_bf = sbL("k_bf", [128, NCH, SEQ], BF16)
                vp = sbL("vp", [128, NTT, NH, 65], BF16)

                with contextlib.ExitStack() as st1:
                    def sb1(name, shape, dt):
                        return st1.enter_context(nc.sbuf_tensor(name + tag, shape, dt))

                    def ps1(name, shape, dt=F32):
                        return st1.enter_context(nc.psum_tensor(name + tag, shape, dt))

                    wq_sb = sb1("wq_sb", [128, NCH, D], BF16)
                    wk_sb = sb1("wk_sb", [128, NCH, D], BF16)
                    wv_sb = sb1("wv_sb", [128, NCH, D], BF16)
                    xband = sb1("xband", [128, NCH, BAND], F32)
                    xb_bf = sb1("xb_bf", [128, NCH, BAND], BF16)
                    sq2 = sb1("sq2", [128, 2, BAND], BF16)
                    xln = sb1("xln", [128, NCH, BAND], BF16)
                    murow = sb1("murow", [1, BAND], F32)
                    varrow = sb1("varrow", [1, BAND], F32)
                    rstdrow = sb1("rstdrow", [1, BAND], F32)
                    mubc = sb1("mubc", [128, BAND], F32)
                    rstdbc = sb1("rstdbc", [128, BAND], F32)
                    ps_q0 = ps1("ps_q0", [128, BAND])
                    ps_q1 = ps1("ps_q1", [128, BAND])
                    ps_sx = ps1("ps_sx", [1, BAND])
                    ps_sxx = ps1("ps_sxx", [1, BAND])
                    ps_bc0 = ps1("ps_bc0", [128, BAND])
                    ps_bc1 = ps1("ps_bc1", [128, BAND])

                    with nc.Block() as blkw:
                        @blkw.gpsimd
                        def _(e):
                            for w_dr, w_s in ((wq, wq_sb), (wk, wk_sb), (wv, wv_sb)):
                                sb.inc(e.dma_start(
                                    out=w_s[:, :, :],
                                    in_=w_dr.ap().rearrange("(c p) o -> p c o", p=128)),
                                    "gd", 16)
                    gd_w = sb.c["gd"]

                    for qb in range(NBAND):
                        with nc.Block() as blk:
                            v_ready = emit_ln_band(
                                blk, src_dram, s, qb * BAND, BAND, xband, xb_bf,
                                sq2, xln, ps_sx, ps_sxx, ps_bc0, ps_bc1,
                                murow, varrow, rstdrow, mubc, rstdbc)

                            rot = Rot(sb, [(ps_q0, "qk0"), (ps_q1, "qk1")])
                            groups = []
                            for wi, (w_s, dstt, bias) in enumerate(
                                    ((wq_sb, q_bf, bq_t), (wk_sb, k_bf, bk_t))):
                                for mc in range(NCH):
                                    groups.append(("qk", w_s, dstt, bias, mc))
                            for tt in range(4):
                                for nh_ in range(2):
                                    groups.append(("v", tt, nh_))

                            mm_qkv = sb.c["mm"]

                            @blk.tensor
                            def _(e, groups=groups, rot=rot, v_ready=v_ready, qb=qb):
                                e.wait_ge(sb.h["v"], v_ready)
                                if qb == 0:
                                    e.wait_ge(sb.h["gd"], gd_w)
                                for g in groups:
                                    bank, bsem = rot.acquire(e)
                                    if g[0] == "qk":
                                        _, w_s, dstt, bias, mc = g
                                        for kk in range(NCH):
                                            sb.inc(e.matmul(
                                                bank[:, :],
                                                w_s[:, kk, mc * 128:(mc + 1) * 128],
                                                xln[:, kk, :], start=(kk == 0),
                                                stop=(kk == NCH - 1)), "mm")
                                    else:
                                        _, tt, nh_ = g
                                        nv = NCH + (1 if l0 else 0)
                                        for kk in range(NCH):
                                            sb.inc(e.matmul(
                                                bank[:, :],
                                                xln[:, kk, tt * 128:(tt + 1) * 128],
                                                wv_sb[:, kk, nh_ * BAND:(nh_ + 1) * BAND],
                                                start=(kk == 0),
                                                stop=(kk == NCH - 1 and not l0)), "mm")
                                        if l0:
                                            sb.inc(e.matmul(
                                                bank[:, :], ones_tok[:, :],
                                                bvrow[0:1, nh_ * BAND:(nh_ + 1) * BAND],
                                                start=False, stop=True), "mm")

                            sc_ev, v_ev = [], []
                            mmc = mm_qkv
                            for gi, g in enumerate(groups):
                                bank, bsem = rot.peek(gi)
                                if g[0] == "qk":
                                    mmc += NCH
                                    sc_ev.append((mmc, bank, bsem, g))
                                else:
                                    mmc += NCH + (1 if l0 else 0)
                                    v_ev.append((mmc, bank, bsem, g))

                            @blk.scalar
                            def _(e, sc_ev=sc_ev, qb=qb):
                                for tgt, bank, bsem, g in sc_ev:
                                    _, w_s, dstt, bias, mc = g
                                    e.wait_ge(sb.h["mm"], tgt)
                                    ii = e.activation(
                                        dstt[:, mc, qb * BAND:(qb + 1) * BAND],
                                        bank[:, :],
                                        AF.Identity if l0 else AF.Copy,
                                        bias=(bias[:, mc:mc + 1] if l0 else 0.0),
                                        scale=1.0)
                                    sb.inc(ii, bsem)

                            @blk.vector
                            def _(e, v_ev=v_ev, qb=qb):
                                for tgt, bank, bsem, g in v_ev:
                                    _, tt, nh_ = g
                                    e.wait_ge(sb.h["mm"], tgt)
                                    ii = e.tensor_copy(
                                        vp[:, qb * 4 + tt, nh_ * 8:(nh_ + 1) * 8, 0:64],
                                        bank[:, :].rearrange("p (h d) -> p h d", h=8))
                                    sb.inc(ii, bsem)

                if stop == 'p1' and layer == 0 and s == 0:
                    with contextlib.ExitStack() as std:
                        stage = std.enter_context(
                            nc.sbuf_tensor("dump_f32", [128, NCH, SEQ], F32))
                        with nc.Block() as blk:
                            @blk.vector
                            def _(e):
                                for c in range(NCH):
                                    ii = e.tensor_copy(stage[:, c, :], q_bf[:, c, :])
                                sb.inc(ii, "v")
                            v_d = sb.c["v"]
                            @blk.sync
                            def _(e, v_d=v_d):
                                sb.wait(e, "v", v_d)
                                sb.inc(e.dma_start(
                                    out=y_out.ap()[0, :, :].rearrange(
                                        "(c p) t -> p c t", p=128),
                                    in_=stage[:, :, :]), "sd", 16)
                        with nc.Block() as blk:
                            @blk.vector
                            def _(e):
                                for c in range(NCH):
                                    ii = e.tensor_copy(stage[:, c, :], k_bf[:, c, :])
                                sb.inc(ii, "v")
                            v_d = sb.c["v"]
                            @blk.sync
                            def _(e, v_d=v_d):
                                sb.wait(e, "v", v_d)
                                sb.inc(e.dma_start(
                                    out=y_out.ap()[1, :, :].rearrange(
                                        "(c p) t -> p c t", p=128),
                                    in_=stage[:, :, :]), "sd", 16)
                    return

                stA = stL.enter_context(contextlib.ExitStack())
                attn_bf = stA.enter_context(
                    nc.sbuf_tensor("attn_bf" + tag, [128, NCH, SEQ], BF16))
                with contextlib.ExitStack() as st2:
                    def sb2(name, shape, dt):
                        return st2.enter_context(nc.sbuf_tensor(name + tag, shape, dt))

                    def ps2(name, shape, dt=F32):
                        return st2.enter_context(nc.psum_tensor(name + tag, shape, dt))

                    pt0 = sb2("pt0", [128, NTT, BAND], BF16)
                    pt1 = sb2("pt1", [128, NTT, BAND], BF16)
                    rpA = sb2("rpA", [1, BAND], F32)
                    rpB = sb2("rpB", [1, BAND], F32)
                    cb_sb = sb2("cb_sb", [128, BAND], F32)
                    sA = [ps2(f"sA{i}", [128, BAND]) for i in range(4)]
                    pB0 = ps2("pB0", [128, BAND])
                    pB1 = ps2("pB1", [128, BAND])
                    pC = ps2("pC", [128, BAND])

                    with nc.Block() as blk:
                        @blk.vector
                        def _(e):
                            e.memset(vp[:, :, :, 64:65], 1.0)

                    bank_free = [0, 0, 0, 0]
                    rotB0 = Rot(sb, [(pB0, "pb0")])
                    rotB1 = Rot(sb, [(pB1, "pb1")])
                    rotC = Rot(sb, [(pC, "cb")])

                    for hp in range(NH // 2):
                        tiles = []
                        mmc = sb.c["mm"]
                        expc = sb.c["exp"]
                        pvkc = sb.c["pvk"]
                        recc = sb.c["rec"]
                        mskc = sb.c["msk"]
                        for qb in range(NBAND):
                            if l0 and qb < 2:
                                kts = list(range(8))
                            elif l0:
                                kts = list(range(4 * (qb + 1)))
                            else:
                                kts = list(range(NTT))
                            masked = [kt for kt in kts
                                      if l0 and qb >= 2 and kt >= 4 * qb]
                            ti = {
                                "qb": qb, "kts": kts, "masked": masked,
                                "mm0": mmc, "exp0": expc, "pvk0": pvkc,
                                "rec0": recc, "msk0": mskc,
                            }
                            ti["mm_s_end"] = mmc + 2 * len(kts)
                            ti["msk_tgt"] = {kt: mskc + mi + 1
                                             for mi, kt in enumerate(masked)}
                            mmc = ti["mm_s_end"] + 2
                            expc += 2 * len(kts)
                            pvkc += 2 * len(kts)
                            recc += 2
                            mskc += len(masked)
                            tiles.append(ti)

                        with nc.Block() as blk:
                            @blk.tensor
                            def _(e, hp=hp, tiles=tiles):
                                for ti in tiles:
                                    qb, kts = ti["qb"], ti["kts"]
                                    qsl = slice(qb * BAND, (qb + 1) * BAND)
                                    for i, kt in enumerate(kts):
                                        j = i % 2
                                        if bank_free[j] > 0:
                                            e.wait_ge(sb.h["exp"], bank_free[j])
                                        sb.inc(e.matmul(
                                            sA[j][:, :],
                                            k_bf[0:64, hp, kt * 128:(kt + 1) * 128],
                                            q_bf[0:64, hp, qsl],
                                            start=True, stop=True,
                                            tile_position=(0, 0)), "mm")
                                        if bank_free[2 + j] > 0:
                                            e.wait_ge(sb.h["exp"], bank_free[2 + j])
                                        sb.inc(e.matmul(
                                            sA[2 + j][:, :],
                                            k_bf[64:128, hp, kt * 128:(kt + 1) * 128],
                                            q_bf[64:128, hp, qsl],
                                            start=True, stop=True,
                                            tile_position=(64, 0)), "mm")
                                        bank_free[j] = ti["exp0"] + 2 * i + 1
                                        bank_free[2 + j] = ti["exp0"] + 2 * i + 2
                                    bank0, _bs0 = rotB0.acquire(e)
                                    bank1, _bs1 = rotB1.acquire(e)
                                    for i, kt in enumerate(kts):
                                        if kt in ti["msk_tgt"]:
                                            e.wait_ge(sb.h["msk"], ti["msk_tgt"][kt])
                                        else:
                                            e.wait_ge(sb.h["exp"],
                                                      ti["exp0"] + 2 * (i + 1))
                                        sb.inc(e.matmul(
                                            bank0[0:65, :], vp[:, kt, 2 * hp, :],
                                            pt0[:, kt, :], start=(i == 0),
                                            stop=(i == len(kts) - 1)), "pvk")
                                        sb.inc(e.matmul(
                                            bank1[0:65, :], vp[:, kt, 2 * hp + 1, :],
                                            pt1[:, kt, :], start=(i == 0),
                                            stop=(i == len(kts) - 1)), "pvk")
                                    bc, _bcs = rotC.acquire(e)
                                    e.wait_ge(sb.h["rec"], ti["rec0"] + 2)
                                    sb.inc(e.matmul(bc[0:64, :], ones_f[0:1, 0:64],
                                                    rpA[:, :], start=True, stop=True),
                                           "mm")
                                    sb.inc(e.matmul(bc[64:128, :], ones_f[0:1, 0:64],
                                                    rpB[:, :], start=True, stop=True),
                                           "mm")

                            @blk.scalar
                            def _(e, hp=hp, tiles=tiles):
                                for tix, ti in enumerate(tiles):
                                    prev = tiles[tix - 1] if tix > 0 else None
                                    for i, kt in enumerate(ti["kts"]):
                                        if prev is not None and kt < len(prev["kts"]):
                                            e.wait_ge(sb.h["pvk"],
                                                      prev["pvk0"] + 2 * (kt + 1))
                                        e.wait_ge(sb.h["mm"], ti["mm0"] + 2 * i + 1)
                                        sb.inc(e.activation(pt0[:, kt, :],
                                                            sA[i % 2][:, :], AF.Exp,
                                                            bias=0.0, scale=0.125),
                                               "exp")
                                        e.wait_ge(sb.h["mm"], ti["mm0"] + 2 * i + 2)
                                        sb.inc(e.activation(pt1[:, kt, :],
                                                            sA[2 + i % 2][:, :], AF.Exp,
                                                            bias=0.0, scale=0.125),
                                               "exp")

                            @blk.vector
                            def _(e, hp=hp, tiles=tiles):
                                for ti in tiles:
                                    qb = ti["qb"]
                                    qsl = slice(qb * BAND, (qb + 1) * BAND)
                                    for kt in ti["masked"]:
                                        j = kt - 4 * qb
                                        i = ti["kts"].index(kt)
                                        e.wait_ge(sb.h["exp"], ti["exp0"] + 2 * (i + 1))
                                        e.tensor_mul(pt0[:, kt, :], pt0[:, kt, :],
                                                     masks[:, j, :])
                                        sb.inc(e.tensor_mul(pt1[:, kt, :],
                                                            pt1[:, kt, :],
                                                            masks[:, j, :]), "msk")
                                    e.wait_ge(sb.h["pvk"],
                                              ti["pvk0"] + 2 * len(ti["kts"]))
                                    sb.inc(e.reciprocal(rpA[:, :], pB0[64:65, :]), "rec")
                                    sb.inc(e.reciprocal(rpB[:, :], pB1[64:65, :]), "rec")
                                    e.wait_ge(sb.h["mm"], ti["mm_s_end"] + 2)
                                    sb.inc(e.tensor_copy(cb_sb[:, :], pC[:, :]), "cb")
                                    sb.inc(e.tensor_mul(
                                        attn_bf[0:64, hp, qsl], pB0[0:64, :],
                                        cb_sb[0:64, :]), "pb0")
                                    sb.inc(e.tensor_mul(
                                        attn_bf[64:128, hp, qsl],
                                        pB1[0:64, :], cb_sb[64:128, :]), "pb1")

                if stop == 'p2' and layer == 0 and s == 0:
                    with contextlib.ExitStack() as std:
                        stage = std.enter_context(
                            nc.sbuf_tensor("dump2_f32", [128, NCH, SEQ], F32))
                        with nc.Block() as blk:
                            @blk.vector
                            def _(e):
                                for c in range(NCH):
                                    ii = e.tensor_copy(stage[:, c, :], attn_bf[:, c, :])
                                sb.inc(ii, "v")
                            v_d = sb.c["v"]
                            @blk.sync
                            def _(e, v_d=v_d):
                                sb.wait(e, "v", v_d)
                                sb.inc(e.dma_start(
                                    out=y_out.ap()[0, :, :].rearrange(
                                        "(c p) t -> p c t", p=128),
                                    in_=stage[:, :, :]), "sd", 16)
                    return

                with contextlib.ExitStack() as st3:
                    def sb3(name, shape, dt):
                        return st3.enter_context(nc.sbuf_tensor(name + tag, shape, dt))

                    def ps3(name, shape, dt=F32):
                        return st3.enter_context(nc.psum_tensor(name + tag, shape, dt))

                    rband = sb3("rband", [128, NCH, BAND], F32)
                    oband = sb3("oband", [128, NCH, BAND], F32)
                    if l0:
                        wo_sb = sb3("wo_sb", [128, NCH, D], BF16)
                        ps_o0 = ps3("ps_o0", [128, BAND])
                        ps_o1 = ps3("ps_o1", [128, BAND])
                        with nc.Block() as blkw:
                            @blkw.gpsimd
                            def _(e):
                                sb.inc(e.dma_start(
                                    out=wo_sb[:, :, :],
                                    in_=wo0.ap().rearrange("(c p) o -> p c o", p=128)),
                                    "gd", 16)
                    gd_wo = sb.c["gd"]

                    for qb in range(NBAND):
                        with nc.Block() as blk:
                            @blk.sync
                            def _(e, qb=qb):
                                sb.inc(e.dma_start(
                                    out=rband[:, :, :],
                                    in_=src_dram.ap()[s, :, qb * BAND:(qb + 1) * BAND]
                                        .rearrange("(c p) t -> p c t", p=128)),
                                    "sd", 16)
                            sd_r = sb.c["sd"]
                            if l0:
                                rotO = Rot(sb, [(ps_o0, "qk0"), (ps_o1, "qk1")])
                                mm_b = sb.c["mm"]

                                @blk.tensor
                                def _(e, qb=qb, rotO=rotO):
                                    if qb == 0:
                                        e.wait_ge(sb.h["gd"], gd_wo)
                                    for mc in range(NCH):
                                        bank, _bs = rotO.acquire(e)
                                        for kk in range(NCH):
                                            sb.inc(e.matmul(
                                                bank[:, :],
                                                wo_sb[:, kk, mc * 128:(mc + 1) * 128],
                                                attn_bf[:, kk,
                                                        qb * BAND:(qb + 1) * BAND],
                                                start=(kk == 0),
                                                stop=(kk == NCH - 1)), "mm")

                                @blk.vector
                                def _(e, qb=qb, mm_b=mm_b, sd_r=sd_r, rotO=rotO):
                                    sb.wait(e, "sd", sd_r)
                                    for mc in range(NCH):
                                        bank, bsem = rotO.peek(mc)
                                        e.wait_ge(sb.h["mm"], mm_b + NCH * (mc + 1))
                                        ii = e.tensor_add(oband[:, mc, :], bank[:, :],
                                                          rband[:, mc, :])
                                        sb.inc(ii, bsem)
                            else:
                                @blk.vector
                                def _(e, qb=qb, sd_r=sd_r):
                                    sb.wait(e, "sd", sd_r)
                                    for mc in range(NCH):
                                        sb.inc(e.tensor_add(
                                            oband[:, mc, :],
                                            attn_bf[:, mc, qb * BAND:(qb + 1) * BAND],
                                            rband[:, mc, :]), "v")
                            qk0_done = sb.c["qk0"]
                            qk1_done = sb.c["qk1"]
                            v_done = sb.c["v"]

                            @blk.sync
                            def _(e, qb=qb, v_done=v_done, qk0_done=qk0_done,
                                  qk1_done=qk1_done, l0=l0):
                                if l0:
                                    sb.wait(e, "qk0", qk0_done)
                                    sb.wait(e, "qk1", qk1_done)
                                else:
                                    sb.wait(e, "v", v_done)
                                sb.inc(e.dma_start(
                                    out=dst_dram.ap()[s, :, qb * BAND:(qb + 1) * BAND]
                                        .rearrange("(c p) t -> p c t", p=128),
                                    in_=oband[:, :, :]), "sd", 16)


        def cc_barrier(k, vals, out_cols):
            with nc.Block() as blk:
                @blk.vector
                def _(e):
                    e.memset(srow[:, :], 0.0)
            for j, val in enumerate(vals):
                with nc.Block() as blk:
                    @blk.sync
                    def _(e, val=val):
                        sb.inc(e.dma_start(out=red_d.ap(), in_=val[:, :],
                                           allow_slow_non_contiguous=True), "sd", 16)
                        sb.wait(e, "sd")
                with nc.Block() as blk:
                    @blk.sync
                    def _(e):
                        sb.inc(e.dma_start(out=redrow[:, :],
                                           in_=red_d.ap().rearrange("p b -> b p")),
                               "sd", 16)
                        sb.wait(e, "sd")
                    sd_now = sb.c["sd"]

                    @blk.vector
                    def _(e, j=j, sd_now=sd_now):
                        sb.wait(e, "sd", sd_now)
                        e.tensor_reduce(srow[0:1, j:j + 1], redrow[:, :],
                                        axis=AX, op=ALU.max)
            with nc.Block() as blk:
                @blk.gpsimd
                def _(e, k=k):
                    sb.inc(e.dma_start(out=cc_in[k].ap(), in_=srow[:, :]), "gd", 16)
                    sb.wait(e, "gd")
                    sb.inc(e.collective_compute(
                        "AllReduce", ALU.max,
                        replica_groups=[list(range(NCORES))],
                        ins=[cc_in[k].ap()],
                        outs=[cc_out[k].ap()]), "cc")
                    sb.wait(e, "cc")
                    sb.inc(e.dma_start(out=srow[:, :], in_=cc_out[k].ap()), "gd", 16)
                    sb.wait(e, "gd")
            with nc.Block() as blk:
                @blk.vector
                def _(e):
                    for j, (sc_c, ic_c) in enumerate(out_cols):
                        sb.inc(e.tensor_scalar(srow[0:1, sc_c:sc_c + 1],
                                               srow[0:1, j:j + 1],
                                               1.0 / QMAX, 1e-12, op0=ALU.mult,
                                               op1=ALU.max), "v")
                        sb.wait(e, "v")
                        sb.inc(e.reciprocal(srow[0:1, ic_c:ic_c + 1],
                                            srow[0:1, sc_c:sc_c + 1]), "v")
                        sb.wait(e, "v")
                v_now = sb.c["v"]

                @blk.tensor
                def _(e, v_now=v_now):
                    e.wait_ge(sb.h["v"], v_now)
                    sb.inc(e.matmul(ps_cc[:, :], ones_f[:, :], srow[:, :],
                                    start=True, stop=True), "mm")
                mm_now = sb.c["mm"]

                @blk.vector
                def _(e, mm_now=mm_now):
                    e.wait_ge(sb.h["mm"], mm_now)
                    for sc_c, ic_c in out_cols:
                        e.tensor_copy(scal[:, sc_c:sc_c + 1], ps_cc[:, sc_c:sc_c + 1])
                        e.tensor_copy(scal[:, ic_c:ic_c + 1], ps_cc[:, ic_c:ic_c + 1])

        def emit_lpbq_weight(nat_dram, dst_sb, nrb, I, wn_f32, wsm, wsc, winv,
                             wu_i, wq_bf, ps_tr):
            nb = I // 32
            inv_b = bass.AP(winv, 0, [[wsm.shape[2] if False else nb, 128],
                                      [1, nb], [0, 32]])
            sc_b = bass.AP(wsc, 0, [[nb, 128], [1, nb], [0, 32]])
            for rb in range(nrb):
                with nc.Block() as blk:
                    @blk.gpsimd
                    def _(e, rb=rb):
                        sb.inc(e.dma_start(
                            out=wn_f32[:, 0:I],
                            in_=nat_dram.ap()[rb * 128:(rb + 1) * 128, :]),
                            "gd", 16)
                    gd_n = sb.c["gd"]

                    @blk.vector
                    def _(e, rb=rb, gd_n=gd_n):
                        sb.wait(e, "gd", gd_n)
                        sb.inc(e.tensor_reduce(
                            wsm[:, 0:nb],
                            wn_f32[:, 0:I].rearrange("p (b q) -> p b q", q=32),
                            axis=AX, op=ALU.max, apply_absolute_value=True), "v")
                        sb.wait(e, "v")
                        sb.inc(e.tensor_scalar(wsc[:, 0:nb], wsm[:, 0:nb],
                                               1.0 / 7.0, 1e-12, op0=ALU.mult,
                                               op1=ALU.max), "v")
                        sb.wait(e, "v")
                        sb.inc(e.reciprocal(winv[:, 0:nb], wsc[:, 0:nb]), "v")
                        sb.wait(e, "v")
                        sb.inc(e.tensor_tensor(
                            wu_i[:, 0:I].rearrange("p (b q) -> p b q", q=32),
                            wn_f32[:, 0:I].rearrange("p (b q) -> p b q", q=32),
                            inv_b, op=ALU.mult), "v")
                        sb.wait(e, "v")
                        sb.inc(e.tensor_tensor(
                            wq_bf[:, 0:I].rearrange("p (b q) -> p b q", q=32),
                            wu_i[:, 0:I].rearrange("p (b q) -> p b q", q=32),
                            sc_b, op=ALU.mult), "v")
                    v_q = sb.c["v"]
                    mm_t = sb.c["mm"]
                    v_e = sb.c["v"]

                    @blk.tensor
                    def _(e, rb=rb, v_q=v_q, mm_t=mm_t):
                        e.wait_ge(sb.h["v"], v_q)
                        for ic in range(I // 128):
                            if ic > 0:
                                e.wait_ge(sb.h["v"], v_q + ic)
                            sb.inc(e.transpose(ps_tr[:, :],
                                               wq_bf[:, ic * 128:(ic + 1) * 128],
                                               ident_bf[:, :]), "mm")

                    @blk.vector
                    def _(e, rb=rb, mm_t=mm_t, v_q=v_q):
                        for ic in range(I // 128):
                            e.wait_ge(sb.h["mm"], mm_t + ic + 1)
                            sb.inc(e.tensor_copy(
                                dst_sb[:, ic, rb * 128:(rb + 1) * 128],
                                ps_tr[:, :]), "v")

        def ffn0(src_dram, dst_dram):
            with contextlib.ExitStack() as stF:
                def sbF(name, shape, dt):
                    return stF.enter_context(nc.sbuf_tensor(name, shape, dt))

                def psF(name, shape, dt=F32):
                    return stF.enter_context(nc.psum_tensor(name, shape, dt))

                global ps_cc
                ps_cc = psF("ps_cc", [128, 16])
                psA = psF("psA", [128, BAND])
                psB = psF("psB", [128, BAND])
                red_tmp = sbF("red_tmp", [128, 1], F32)

                with nc.Block() as blk:
                    @blk.vector
                    def _(e):
                        e.memset(gmax_t[:, :], 0.0)
                        e.memset(amax_t[:, :], 0.0)
                        e.memset(red_buf[:, :], 0.0)

                stW = contextlib.ExitStack()
                st1 = contextlib.ExitStack()

                def sb1(name, shape, dt):
                    return st1.enter_context(nc.sbuf_tensor(name, shape, dt))

                def ps1(name, shape, dt=F32):
                    return st1.enter_context(nc.psum_tensor(name, shape, dt))

                if True:
                    ln_bf = stW.enter_context(nc.sbuf_tensor("ln_bf", [128, NCH, TTOK], BF16))
                    wg_sb = stW.enter_context(nc.sbuf_tensor("wg_sb", [128, NCH, DFF], BF16))
                    wu_sb = stW.enter_context(nc.sbuf_tensor("wu_sb", [128, NCH, DFF], BF16))

                    stD = contextlib.ExitStack()
                    wn_f32 = stD.enter_context(nc.sbuf_tensor("wn_f32", [128, D], F32))
                    wsm = stD.enter_context(nc.sbuf_tensor("wsm", [128, D // 32], F32))
                    wsc = stD.enter_context(nc.sbuf_tensor("wsc", [128, D // 32], F32))
                    winv = stD.enter_context(nc.sbuf_tensor("winv", [128, D // 32], F32))
                    wu_i = stD.enter_context(nc.sbuf_tensor("wu_i", [128, D], I32))
                    wq_bf = stD.enter_context(nc.sbuf_tensor("wq_bf", [128, D], BF16))
                    ps_tr = stD.enter_context(nc.psum_tensor("ps_tr", [128, 128], BF16))
                    emit_lpbq_weight(wu_nat, wu_sb, NMC, D, wn_f32, wsm, wsc,
                                     winv, wu_i, wq_bf, ps_tr)
                    emit_lpbq_weight(wg_nat, wg_sb, NMC, D, wn_f32, wsm, wsc,
                                     winv, wu_i, wq_bf, ps_tr)
                    stD.close()
                    gd_wf = sb.c["gd"]

                    xband = sb1("f1_xband", [128, NCH, BAND], F32)
                    xb_bf = sb1("f1_xb", [128, NCH, BAND], BF16)
                    sq2 = sb1("f1_sq2", [128, 2, BAND], BF16)
                    murow = sb1("f1_mu", [1, BAND], F32)
                    varrow = sb1("f1_var", [1, BAND], F32)
                    rstdrow = sb1("f1_rstd", [1, BAND], F32)
                    mubc = sb1("f1_mubc", [128, BAND], F32)
                    rstdbc = sb1("f1_rstdbc", [128, BAND], F32)
                    ps_sx = ps1("f1_sx", [1, BAND])
                    ps_sxx = ps1("f1_sxx", [1, BAND])

                    for gb in range(NGB):
                        s_, qb = gb // NBAND, gb % NBAND
                        with nc.Block() as blk:
                            v_ready = emit_ln_band(
                                blk, src_dram, s_, qb * BAND, BAND, xband, xb_bf,
                                sq2, ln_bf, ps_sx, ps_sxx, psA, psB,
                                murow, varrow, rstdrow, mubc, rstdbc,
                                xcol0=gb * BAND)

                            @blk.vector
                            def _(e, gb=gb, v_ready=v_ready):
                                e.wait_ge(sb.h["v"], v_ready)
                                for c in range(NCH):
                                    e.tensor_reduce(
                                        red_buf[:, gb * NCH + c:gb * NCH + c + 1],
                                        ln_bf[:, c, gb * BAND:(gb + 1) * BAND],
                                        axis=AX, op=ALU.max,
                                        apply_absolute_value=True)

                    st1.close()
                    with nc.Block() as blk:
                        @blk.vector
                        def _(e):
                            sb.inc(e.tensor_reduce(gmax_t[:, :], red_buf[:, :],
                                                   axis=AX, op=ALU.max), "v")
                            sb.wait(e, "v")
                            e.memset(red_buf[:, :], 0.0)
                    cc_barrier(0, [gmax_t], [(0, 1)])

                    with nc.Block() as blk:
                        @blk.vector
                        def _(e):
                            e.memset(gmax_t[:, :], 0.0)
                    st2 = contextlib.ExitStack()
                    xq_i32 = st2.enter_context(nc.sbuf_tensor("xq_i32", [128, BAND], I32))
                    xq_bf = st2.enter_context(nc.sbuf_tensor("xq_bf", [128, NCH, BAND], BF16))
                    stg = st2.enter_context(nc.sbuf_tensor("f2_stg", [128, 4, BAND], BF16))

                    for gb in range(NGB):
                        gsl = slice(gb * BAND, (gb + 1) * BAND)
                        with nc.Block() as blk:
                            v0 = sb.c["v"]

                            @blk.vector
                            def _(e, gb=gb, gsl=gsl):
                                if gb == 0:
                                    e.wait_ge(sb.h["gd"], gd_wf)
                                for c in range(NCH):
                                    e.tensor_scalar(xq_i32[:, :], ln_bf[:, c, gsl],
                                                    scal[:, 1:2], None, op0=ALU.mult)
                                    sb.inc(e.tensor_copy(xq_bf[:, c, :],
                                                         xq_i32[:, :]), "v")

                            mmb = sb.c["mm"]
                            scb = sb.c["sc"]
                            f2b = sb.c["f2"]
                            sdb = sb.c["sd"]
                            groups = [(w, mc) for w in range(2) for mc in range(NMC)]

                            @blk.tensor
                            def _(e, v0=v0, groups=groups, scb=scb):
                                e.wait_ge(sb.h["v"], v0 + NCH)
                                for G, (w, mc) in enumerate(groups):
                                    w_s = wu_sb if w == 0 else wg_sb
                                    bank = psA if G % 2 == 0 else psB
                                    if G >= 2:
                                        e.wait_ge(sb.h["sc"], scb + G - 1)
                                    for kk in range(NCH):
                                        sb.inc(e.matmul(
                                            bank[:, :],
                                            w_s[:, kk, mc * 128:(mc + 1) * 128],
                                            xq_bf[:, kk, :], start=(kk == 0),
                                            stop=(kk == NCH - 1)), "mm")

                            @blk.scalar
                            def _(e, groups=groups, mmb=mmb, f2b=f2b, sdb=sdb):
                                for G, (w, mc) in enumerate(groups):
                                    bank = psA if G % 2 == 0 else psB
                                    if G >= 4:
                                        e.wait_ge(sb.h["f2"], f2b + G - 3)
                                        e.wait_ge(sb.h["sd"], sdb + 16 * (G - 3))
                                    e.wait_ge(sb.h["mm"], mmb + NCH * (G + 1))
                                    sb.inc(e.activation(stg[:, G % 4, :], bank[:, :],
                                                        AF.Identity, bias=0.0,
                                                        scale=scal[:, 0:1]), "sc")

                            @blk.vector
                            def _(e, groups=groups, scb=scb, gb=gb):
                                for G, (w, mc) in enumerate(groups):
                                    e.wait_ge(sb.h["sc"], scb + G + 1)
                                    col = w * 192 + mc * NGB + gb
                                    sb.inc(e.tensor_reduce(
                                        red_buf[:, col:col + 1], stg[:, G % 4, :],
                                        axis=AX, op=ALU.max,
                                        apply_absolute_value=True), "f2")

                            @blk.sync
                            def _(e, groups=groups, scb=scb, gb=gb, gsl=gsl):
                                for G, (w, mc) in enumerate(groups):
                                    e.wait_ge(sb.h["sc"], scb + G + 1)
                                    dstd = up_d if w == 0 else gate_d
                                    sb.inc(e.dma_start(
                                        out=dstd.ap()[mc, :, gsl],
                                        in_=stg[:, G % 4, :]), "sd", 16)

                    st2.close()
                    stW.close()

                with nc.Block() as blk:
                    @blk.vector
                    def _(e):
                        sb.inc(e.tensor_reduce(gmax_t[:, :], red_buf[:, 0:192],
                                               axis=AX, op=ALU.max), "v")
                        sb.wait(e, "v")
                        sb.inc(e.tensor_reduce(amax_t[:, :], red_buf[:, 192:384],
                                               axis=AX, op=ALU.max), "v")
                        sb.wait(e, "v")
                        e.memset(red_buf[:, :], 0.0)
                cc_barrier(1, [gmax_t, amax_t], [(2, 3), (4, 5)])

                with contextlib.ExitStack() as st3:
                    def sb3(name, shape, dt):
                        return st3.enter_context(nc.sbuf_tensor(name, shape, dt))

                    with nc.Block() as blk:
                        @blk.vector
                        def _(e):
                            e.memset(gmax_t[:, :], 0.0)

                    gt = sb3("f3_gt", [128, 4, BAND], BF16)
                    g_i = sb3("f3_gi", [128, BAND], I32)
                    g_q = sb3("f3_gq", [128, BAND], F32)
                    sg_f = sb3("f3_sg", [128, BAND], F32)
                    s_i = sb3("f3_si", [128, BAND], I32)
                    s_q = sb3("f3_sq", [128, BAND], F32)
                    pstg = sb3("f3_ps", [128, 4, BAND], BF16)

                    ntile = NMC * NGB
                    with nc.Block() as blk:
                        sd0 = sb.c["sd"]
                        v0 = sb.c["v"]
                        sc0 = sb.c["sc"]
                        f20 = sb.c["f2"]

                        @blk.sync
                        def _(e, sd0=sd0, v0=v0):
                            for T in range(ntile):
                                mc, gb = T // NGB, T % NGB
                                if T >= 4:
                                    e.wait_ge(sb.h["v"], v0 + 2 * (T - 3))
                                sb.inc(e.dma_start(
                                    out=gt[:, T % 4, :],
                                    in_=gate_d.ap()[mc, :,
                                                    gb * BAND:(gb + 1) * BAND]),
                                    "sd", 16)
                                e.wait_ge(sb.h["f2"], f20 + T + 1)
                                sb.inc(e.dma_start(
                                    out=prod_d.ap()[mc, :,
                                                    gb * BAND:(gb + 1) * BAND],
                                    in_=pstg[:, T % 4, :]), "sd", 16)

                        @blk.vector
                        def _(e, sd0=sd0, sc0=sc0):
                            for T in range(ntile):
                                e.wait_ge(sb.h["sd"], sd0 + 32 * T + 16)
                                e.tensor_scalar(g_i[:, :], gt[:, T % 4, :],
                                                scal[:, 5:6], None, op0=ALU.mult)
                                sb.inc(e.tensor_scalar(g_q[:, :], g_i[:, :],
                                                       scal[:, 4:5], None,
                                                       op0=ALU.mult), "v")
                                e.wait_ge(sb.h["sc"], sc0 + T + 1)
                                e.tensor_scalar(s_i[:, :], sg_f[:, :], 65536.0,
                                                65535.0, op0=ALU.mult, op1=ALU.min)
                                e.tensor_scalar(s_q[:, :], s_i[:, :],
                                                float(2.0 ** -16), None,
                                                op0=ALU.mult)
                                sb.inc(e.tensor_mul(pstg[:, T % 4, :], g_q[:, :],
                                                    s_q[:, :]), "v")
                                sb.inc(e.tensor_reduce(
                                    red_buf[:, T:T + 1], pstg[:, T % 4, :],
                                    axis=AX, op=ALU.max,
                                    apply_absolute_value=True), "f2")

                        @blk.scalar
                        def _(e, v0=v0):
                            for T in range(ntile):
                                e.wait_ge(sb.h["v"], v0 + 2 * T + 1)
                                sb.inc(e.activation(sg_f[:, :], g_q[:, :],
                                                    AF.Sigmoid, bias=0.0,
                                                    scale=1.0), "sc")

                with nc.Block() as blk:
                    @blk.vector
                    def _(e):
                        sb.inc(e.tensor_reduce(gmax_t[:, :], red_buf[:, 0:192],
                                               axis=AX, op=ALU.max), "v")
                        sb.wait(e, "v")
                        e.memset(red_buf[:, :], 0.0)
                cc_barrier(2, [gmax_t], [(6, 7)])

                with contextlib.ExitStack() as st4:
                    def sb4(name, shape, dt):
                        return st4.enter_context(nc.sbuf_tensor(name, shape, dt))

                    with nc.Block() as blk:
                        @blk.vector
                        def _(e):
                            e.memset(amax_t[:, :], 0.0)

                    pt_ = sb4("f4_pt", [128, 4, BAND], BF16)
                    ut_ = sb4("f4_ut", [128, 4, BAND], BF16)
                    p_i = sb4("f4_pi", [128, BAND], I32)
                    p_q = sb4("f4_pq", [128, BAND], F32)
                    u_i = sb4("f4_ui", [128, BAND], I32)
                    u_q = sb4("f4_uq", [128, BAND], F32)
                    ostg = sb4("f4_os", [128, 4, BAND], BF16)

                    ntile = NMC * NGB
                    with nc.Block() as blk:
                        sd0 = sb.c["sd"]
                        v0 = sb.c["v"]
                        f20 = sb.c["f2"]

                        @blk.sync
                        def _(e, sd0=sd0, v0=v0, f20=f20):
                            for T in range(ntile):
                                mc, gb = T // NGB, T % NGB
                                gsl = slice(gb * BAND, (gb + 1) * BAND)
                                if T >= 4:
                                    e.wait_ge(sb.h["v"], v0 + T - 3)
                                sb.inc(e.dma_start(out=pt_[:, T % 4, :],
                                                   in_=prod_d.ap()[mc, :, gsl]),
                                       "sd", 16)
                                sb.inc(e.dma_start(out=ut_[:, T % 4, :],
                                                   in_=up_d.ap()[mc, :, gsl]),
                                       "sd", 16)
                                e.wait_ge(sb.h["f2"], f20 + T + 1)
                                sb.inc(e.dma_start(out=o_d.ap()[mc, :, gsl],
                                                   in_=ostg[:, T % 4, :]), "sd", 16)

                        @blk.vector
                        def _(e, sd0=sd0):
                            for T in range(ntile):
                                e.wait_ge(sb.h["sd"], sd0 + 48 * T + 32)
                                e.tensor_scalar(p_i[:, :], pt_[:, T % 4, :],
                                                scal[:, 7:8], None, op0=ALU.mult)
                                e.tensor_scalar(p_q[:, :], p_i[:, :],
                                                scal[:, 6:7], None, op0=ALU.mult)
                                e.tensor_scalar(u_i[:, :], ut_[:, T % 4, :],
                                                scal[:, 3:4], None, op0=ALU.mult)
                                e.tensor_scalar(u_q[:, :], u_i[:, :],
                                                scal[:, 2:3], None, op0=ALU.mult)
                                sb.inc(e.tensor_mul(ostg[:, T % 4, :], p_q[:, :],
                                                    u_q[:, :]), "v")
                                sb.inc(e.tensor_reduce(
                                    red_buf[:, T:T + 1], ostg[:, T % 4, :],
                                    axis=AX, op=ALU.max,
                                    apply_absolute_value=True), "f2")

                with nc.Block() as blk:
                    @blk.vector
                    def _(e):
                        sb.inc(e.tensor_reduce(amax_t[:, :], red_buf[:, 0:192],
                                               axis=AX, op=ALU.max), "v")
                        sb.wait(e, "v")
                cc_barrier(3, [amax_t], [(8, 9)])

                with contextlib.ExitStack() as st5:
                    def sb5(name, shape, dt):
                        return st5.enter_context(nc.sbuf_tensor(name, shape, dt))

                    wd_sb = sb5("wd_sb", [128, NMC, D], BF16)
                    wn5 = sb5("wn5", [128, DFF], F32)
                    wsm5 = sb5("wsm5", [128, DFF // 32], F32)
                    wsc5 = sb5("wsc5", [128, DFF // 32], F32)
                    winv5 = sb5("winv5", [128, DFF // 32], F32)
                    wu_i5 = sb5("wu_i5", [128, DFF], I32)
                    wq_bf5 = sb5("wq_bf5", [128, DFF], BF16)
                    ps_tr5 = st5.enter_context(nc.psum_tensor("ps_tr5", [128, 128], BF16))
                    emit_lpbq_weight(wd_nat, wd_sb, NCH, DFF, wn5, wsm5, wsc5,
                                     winv5, wu_i5, wq_bf5, ps_tr5)
                    gd_wd = sb.c["gd"]

                    ot = sb5("f5_ot", [128, NMC, BAND], BF16)
                    o_i = sb5("f5_oi", [128, BAND], I32)
                    ou_bf = sb5("f5_ou", [128, NMC, BAND], BF16)
                    rband = sb5("f5_rb", [128, NCH, BAND], F32)
                    oband = sb5("f5_ob", [128, NCH, BAND], F32)

                    for gb in range(NGB):
                        s_, qb = gb // NBAND, gb % NBAND
                        gsl = slice(gb * BAND, (gb + 1) * BAND)
                        with nc.Block() as blk:
                            @blk.sync
                            def _(e, gb=gb, gsl=gsl, s_=s_, qb=qb):
                                sb.inc(e.dma_start(
                                    out=ot[:, :, :],
                                    in_=o_d.ap()[:, :, gsl]
                                        .rearrange("c p t -> p c t")), "sd", 16)
                                sb.inc(e.dma_start(
                                    out=rband[:, :, :],
                                    in_=src_dram.ap()[s_, :,
                                                      qb * BAND:(qb + 1) * BAND]
                                        .rearrange("(c p) t -> p c t", p=128)),
                                    "sd", 16)
                            sd_l = sb.c["sd"]
                            v0 = sb.c["v"]

                            @blk.vector
                            def _(e, sd_l=sd_l, gb=gb):
                                sb.wait(e, "sd", sd_l - 16)
                                if gb == 0:
                                    e.wait_ge(sb.h["gd"], gd_wd)
                                for kk in range(NMC):
                                    e.tensor_scalar(o_i[:, :], ot[:, kk, :],
                                                    scal[:, 9:10], None,
                                                    op0=ALU.mult)
                                    sb.inc(e.tensor_copy(ou_bf[:, kk, :],
                                                         o_i[:, :]), "v")

                            rotD = Rot(sb, [(psA, "f0"), (psB, "f1")])
                            mmb = sb.c["mm"]

                            @blk.tensor
                            def _(e, v0=v0, rotD=rotD):
                                for mc in range(NCH):
                                    bank, _bs = rotD.acquire(e)
                                    for kk in range(NMC):
                                        if mc == 0:
                                            e.wait_ge(sb.h["v"], v0 + kk + 1)
                                        sb.inc(e.matmul(
                                            bank[:, :],
                                            wd_sb[:, kk, mc * 128:(mc + 1) * 128],
                                            ou_bf[:, kk, :], start=(kk == 0),
                                            stop=(kk == NMC - 1)), "mm")

                            @blk.vector
                            def _(e, mmb=mmb, sd_l=sd_l, rotD=rotD):
                                sb.wait(e, "sd", sd_l)
                                for mc in range(NCH):
                                    bank, bsem = rotD.peek(mc)
                                    e.wait_ge(sb.h["mm"], mmb + NMC * (mc + 1))
                                    ii = e.scalar_tensor_tensor(
                                        oband[:, mc, :], bank[:, :], scal[:, 8:9],
                                        rband[:, mc, :], op0=ALU.mult, op1=ALU.add)
                                    sb.inc(ii, bsem)
                            f0_d = sb.c["f0"]
                            f1_d = sb.c["f1"]

                            @blk.sync
                            def _(e, s_=s_, qb=qb, f0_d=f0_d, f1_d=f1_d):
                                sb.wait(e, "f0", f0_d)
                                sb.wait(e, "f1", f1_d)
                                sb.inc(e.dma_start(
                                    out=dst_dram.ap()[s_, :,
                                                      qb * BAND:(qb + 1) * BAND]
                                        .rearrange("(c p) t -> p c t", p=128),
                                    in_=oband[:, :, :]), "sd", 16)


        def ffn1(src_dram, dst_dram):
            QT = 1024
            NSUP = 8
            KSUP = 16
            with contextlib.ExitStack() as stF:
                def sbF(name, shape, dt):
                    return stF.enter_context(nc.sbuf_tensor(name, shape, dt))

                def psF(name, shape, dt=F32):
                    return stF.enter_context(nc.psum_tensor(name, shape, dt))

                rm_sb = sbF("rm_sb", [128, 32, 128], BF16)
                rup_sb = sbF("rup_sb", [128, 128, 128], BF16)
                psA = psF("g_psA", [128, BAND])
                psB = psF("g_psB", [128, BAND])
                psC0 = psF("g_psC0", [128, BAND])
                psC1 = psF("g_psC1", [128, BAND])
                ps_sx = psF("g_sx", [1, BAND])
                ps_sxx = psF("g_sxx", [1, BAND])

                with nc.Block() as blkw:
                    @blkw.gpsimd
                    def _(e):
                        sb.inc(e.dma_start(
                            out=rm_sb[:, :, :],
                            in_=rmbd_d.ap().rearrange("h g p j -> p (h g) j")),
                            "gd", 16)
                        sb.inc(e.dma_start(
                            out=rup_sb[:, 0:64, :],
                            in_=rupbd_d.ap()[0:2]
                                .rearrange("h g m p j -> p (h g m) j")), "gd", 16)
                        sb.inc(e.dma_start(
                            out=rup_sb[:, 64:128, :],
                            in_=rupbd_d.ap()[2:4]
                                .rearrange("h g m p j -> p (h g m) j")), "gd", 16)
                gd_rm = sb.c["gd"]

                for qt in range(4):
                    s_, hf = qt // 2, qt % 2
                    tq0 = hf * QT
                    with contextlib.ExitStack() as stQ:
                        def sbQ(name, shape, dt):
                            return stQ.enter_context(nc.sbuf_tensor(name, shape, dt))

                        xr_bf = sbQ(f"xr{qt}", [128, NCH, QT], BF16)
                        acc = sbQ(f"acc{qt}", [128, NCH, QT], F32)
                        h_sup = sbQ(f"hs{qt}", [128, KSUP, QT], BF16)
                        y_t = sbQ(f"yt{qt}", [128, QT], BF16)

                        with contextlib.ExitStack() as stL:
                            def sbL2(name, shape, dt):
                                return stL.enter_context(
                                    nc.sbuf_tensor(name + f"_q{qt}", shape, dt))

                            xband = sbL2("g1_xband", [128, NCH, BAND], F32)
                            xb_bf = sbL2("g1_xb", [128, NCH, BAND], BF16)
                            sq2 = sbL2("g1_sq2", [128, 2, BAND], BF16)
                            murow = sbL2("g1_mu", [1, BAND], F32)
                            varrow = sbL2("g1_var", [1, BAND], F32)
                            rstdrow = sbL2("g1_rstd", [1, BAND], F32)
                            mubc = sbL2("g1_mubc", [128, BAND], F32)
                            rstdbc = sbL2("g1_rstdbc", [128, BAND], F32)
                            for bb in range(2):
                                with nc.Block() as blk:
                                    emit_ln_band(
                                        blk, src_dram, s_, tq0 + bb * BAND, BAND,
                                        xband, xb_bf, sq2, xr_bf, ps_sx, ps_sxx,
                                        psA, psB, murow, varrow, rstdrow, mubc,
                                        rstdbc, xcol0=bb * BAND)

                        with contextlib.ExitStack() as stC:
                            wc_sb = stC.enter_context(nc.sbuf_tensor(
                                f"wc{qt}", [128, 2, KSUP, D], BF16))

                            plan = []
                            _cp_end = {}
                            mm_c = 0
                            for sup in range(NSUP):
                                prod = []
                                for hg in range(4):
                                    idx0 = sup * KSUP + hg * 4
                                    h_ = idx0 // 32
                                    g = (idx0 // 4) % 8
                                    prod.append((h_, g, idx0))
                                plan.append(prod)
                                mm_c += 4 * (2 + 8) + KSUP * 16
                                _cp_end[sup] = mm_c

                            with nc.Block() as blk:
                                gd0 = sb.c["gd"]
                                mm0 = sb.c["mm"]
                                sc0 = sb.c["sc"]
                                v0 = sb.c["v"]
                                f00 = sb.c["f0"]

                                @blk.gpsimd
                                def _(e, gd0=gd0, mm0=mm0, qt=qt):
                                    for sup in range(NSUP):
                                        if sup >= 2:
                                            e.wait_ge(sb.h["mm"],
                                                      mm0 + _cp_end[sup - 2])
                                        r0 = sup * KSUP * 128
                                        sb.inc(e.dma_start(
                                            out=wc_sb[:, sup % 2, :, :],
                                            in_=wcT.ap()[r0:r0 + KSUP * 128, :]
                                                .rearrange("(c p) o -> p c o",
                                                           p=128)), "gd", 16)

                                @blk.tensor
                                def _(e, gd0=gd0, sc0=sc0, v0=v0, f00=f00):
                                    sc_n = sc0
                                    vy_n = v0
                                    gc = 0
                                    for sup in range(NSUP):
                                        if sup == 0:
                                            e.wait_ge(sb.h["gd"], gd_rm)
                                        e.wait_ge(sb.h["gd"], gd0 + 16 * (sup + 1))
                                        for h_, g, idx0 in plan[sup]:
                                            if sup > 0 or idx0 > 0:
                                                e.wait_ge(sb.h["sc"], sc_n)
                                            for nh_ in range(2):
                                                bank = psA if nh_ == 0 else psB
                                                sb.inc(e.matmul(
                                                    bank[:, :],
                                                    rm_sb[:, h_ * 8 + g, :],
                                                    xr_bf[:, g,
                                                          nh_ * BAND:(nh_ + 1) * BAND],
                                                    start=True, stop=True), "mm")
                                            vy_n += 1
                                            for mp in range(4):
                                                for nh_ in range(2):
                                                    bank = psA if nh_ == 0 else psB
                                                    if mp == 0 and nh_ == 0:
                                                        e.wait_ge(sb.h["v"], vy_n)
                                                    if mp >= 1:
                                                        e.wait_ge(
                                                            sb.h["sc"],
                                                            sc_n + 2 * (mp - 1) + nh_ + 1)
                                                    sb.inc(e.matmul(
                                                        bank[:, :],
                                                        rup_sb[:, (h_ * 8 + g) * 4 + mp, :],
                                                        y_t[:, nh_ * BAND:(nh_ + 1) * BAND],
                                                        start=True, stop=True), "mm")
                                            sc_n += 8
                                        e.wait_ge(sb.h["sc"], sc_n)
                                        for mc in range(NCH):
                                            for nq in range(2):
                                                cb = psC0 if gc % 2 == 0 else psC1
                                                if gc >= 2:
                                                    e.wait_ge(sb.h["f0"],
                                                              f00 + gc - 1)
                                                gc += 1
                                                for idx in range(KSUP):
                                                    sb.inc(e.matmul(
                                                        cb[:, :],
                                                        wc_sb[:, sup % 2, idx,
                                                              mc * 128:(mc + 1) * 128],
                                                        h_sup[:, idx,
                                                              nq * BAND:(nq + 1) * BAND],
                                                        start=(idx == 0),
                                                        stop=(idx == KSUP - 1)),
                                                        "mm")

                                @blk.vector
                                def _(e, mm0=mm0):
                                    mm_n = mm0
                                    for sup in range(NSUP):
                                        for h_, g, idx0 in plan[sup]:
                                            mm_n += 2
                                            e.wait_ge(sb.h["mm"], mm_n - 1)
                                            e.tensor_copy(y_t[:, 0:BAND], psA[:, :])
                                            e.wait_ge(sb.h["mm"], mm_n)
                                            sb.inc(e.tensor_copy(y_t[:, BAND:QT],
                                                                 psB[:, :]), "v")
                                            mm_n += 8
                                        for mc in range(NCH):
                                            for nq in range(2):
                                                cb = psC0 if (mc * 2 + nq) % 2 == 0 else psC1
                                                mm_n += KSUP
                                                e.wait_ge(sb.h["mm"], mm_n)
                                                asl = acc[:, mc,
                                                          nq * BAND:(nq + 1) * BAND]
                                                if sup == 0:
                                                    sb.inc(e.tensor_copy(asl, cb[:, :]),
                                                           "f0")
                                                else:
                                                    sb.inc(e.tensor_add(asl, asl,
                                                                        cb[:, :]),
                                                           "f0")

                                @blk.scalar
                                def _(e, mm0=mm0):
                                    mm_n = mm0
                                    for sup in range(NSUP):
                                        for h_, g, idx0 in plan[sup]:
                                            mm_n += 2
                                            for mp in range(4):
                                                idx = hg4 = (idx0 - sup * KSUP) + mp
                                                for nh_ in range(2):
                                                    bank = psA if nh_ == 0 else psB
                                                    mm_n += 1
                                                    e.wait_ge(sb.h["mm"], mm_n)
                                                    sb.inc(e.activation(
                                                        h_sup[:, idx,
                                                              nh_ * BAND:(nh_ + 1) * BAND],
                                                        bank[:, :], AF.Gelu,
                                                        bias=0.0, scale=1.0), "sc")
                                        mm_n += KSUP * 16

                        with contextlib.ExitStack() as stR:
                            rq = stR.enter_context(nc.sbuf_tensor(
                                f"rq{qt}", [128, NCH, QT], F32))
                            with nc.Block() as blk:
                                @blk.sync
                                def _(e, s_=s_, tq0=tq0):
                                    sb.inc(e.dma_start(
                                        out=rq[:, :, :],
                                        in_=src_dram.ap()[s_, :, tq0:tq0 + QT]
                                            .rearrange("(c p) t -> p c t", p=128)),
                                        "sd", 16)
                                sd_r = sb.c["sd"]

                                @blk.vector
                                def _(e, sd_r=sd_r):
                                    sb.wait(e, "sd", sd_r)
                                    for mc in range(NCH):
                                        ii = e.tensor_add(acc[:, mc, :],
                                                          acc[:, mc, :],
                                                          rq[:, mc, :])
                                    sb.inc(ii, "v")
                                v_d = sb.c["v"]

                                @blk.sync
                                def _(e, s_=s_, tq0=tq0, v_d=v_d):
                                    sb.wait(e, "v", v_d)
                                    sb.inc(e.dma_start(
                                        out=dst_dram.ap()[s_, :, tq0:tq0 + QT]
                                            .rearrange("(c p) t -> p c t", p=128),
                                        in_=acc[:, :, :]), "sd", 16)

        dstA = y_out if stop == 'l0a' else resA
        for s in range((1 if stop in ('p1', 'p2') else 2)):
            attn_layer(0, s, x_in, dstA)

        if stop not in ('l0a', 'p1', 'p2'):
            ffn0(resA, y_out if stop == 'l0f' else resB)
        if stop in ('l1a', 'full'):
            for s in range(2):
                attn_layer(1, s, resB, y_out if stop == 'l1a' else resC)
        if stop == 'full':
            ffn1(resC, y_out)

        with nc.Block() as blk:
            @blk.sync
            def _(e):
                sb.wait(e, "sd")
            @blk.gpsimd
            def _(e):
                sb.wait(e, "gd")

    return nc


def _host_prep(inputs):
    x = np.asarray(inputs["L0_attn_x"])
    base = {
        "wq0": np.ascontiguousarray(np.asarray(inputs["L0_attn_Wq"], dtype=np.float32)),
        "wk0": np.ascontiguousarray(np.asarray(inputs["L0_attn_Wk"], dtype=np.float32)),
        "wv0": np.ascontiguousarray(np.asarray(inputs["L0_attn_Wv"], dtype=np.float32)),
        "wo0": np.ascontiguousarray(np.asarray(inputs["L0_attn_Wo"], dtype=np.float32)),
        "bq0": np.ascontiguousarray(np.asarray(inputs["L0_attn_bq"], dtype=np.float32).reshape(NCH, 128).T),
        "bk0": np.ascontiguousarray(np.asarray(inputs["L0_attn_bk"], dtype=np.float32).reshape(NCH, 128).T),
        "bv0r": np.ascontiguousarray(np.asarray(inputs["L0_attn_bv"], dtype=np.float32).reshape(1, D)),
        "wgT": np.ascontiguousarray(np.asarray(inputs["L0_ffn_w_gate"], dtype=np.float32)),
        "wuT": np.ascontiguousarray(np.asarray(inputs["L0_ffn_w_up"], dtype=np.float32)),
        "wdT": np.ascontiguousarray(np.asarray(inputs["L0_ffn_w_down"], dtype=np.float32)),
        "wcT": np.ascontiguousarray(np.asarray(inputs["L1_ffn_w_cproj"], dtype=np.float32).T),
    }
    m = np.asarray(inputs["L1_attn_MhWqkv"], dtype=np.float32).reshape(D, NH, 3, 64)
    base["wq1"] = np.ascontiguousarray(m[:, :, 0, :].reshape(D, D))
    base["wk1"] = np.ascontiguousarray(m[:, :, 1, :].reshape(D, D))
    base["wv1"] = np.ascontiguousarray(m[:, :, 2, :].reshape(D, D))
    rm = np.asarray(inputs["L1_ffn_rail_maker"], dtype=np.float32)
    rmbd = np.zeros((4, 8, 128, 128), dtype=np.float32)
    for g in range(8):
        for rp in range(8):
            rmbd[:, g, 16 * rp:16 * rp + 16, 16 * rp:16 * rp + 16] = rm[:, 8 * g + rp]
    base["rmbd"] = rmbd
    rup = np.asarray(inputs["L1_ffn_rail_up_proj"], dtype=np.float32)
    rupbd = np.zeros((4, 8, 4, 128, 128), dtype=np.float32)
    for g in range(8):
        for rp in range(8):
            mp, a = rp // 2, rp % 2
            rupbd[:, g, mp, 16 * rp:16 * rp + 16, 64 * a:64 * a + 64] = \
                np.transpose(rup[:, 8 * g + rp], (0, 2, 1))
    base["rupbd"] = rupbd

    per_core = []
    for c in range(NCORES):
        d = dict(base)
        d["x"] = np.ascontiguousarray(
            np.asarray(x[2 * c:2 * c + 2], dtype=np.float32).transpose(0, 2, 1))
        per_core.append(d)
    return per_core


def _run(inputs, stop='full'):
    in_maps = _host_prep(inputs)
    nc = _build(stop)
    res = run_bass_kernel_spmd(nc, in_maps, core_ids=list(range(NCORES)))
    out = np.empty((16, SEQ, D), dtype=np.float32)
    for c in range(NCORES):
        out[2 * c:2 * c + 2] = res.results[c]["y"].transpose(0, 2, 1)
    return out


def kernel(**inputs):
    return _run(inputs, 'full')
